# version 02 (final)
# baseline (speedup 1.0000x reference)
import contextlib
import numpy as np
import ml_dtypes
import concourse.bass as bass
import concourse.mybir as mybir
from concourse.bass_utils import run_bass_kernel_spmd

F32 = mybir.dt.float32
BF16 = mybir.dt.bfloat16
I32 = mybir.dt.int32
AF = mybir.ActivationFunctionType
OP = mybir.AluOpType
AX = mybir.AxisListType

NCORES = 8
BATCH = 16
B = 2
S = 2048
T = B * S
D = 1024
DCH = 8
DI = 2048
DICH = 16
DS = 16
IF = 2816
IFCH = 22
NT = 512
NTT = T // NT
QMAX = 127.0
EPS = 1e-5
TC = 256
NTC = S // TC
POOLW = 26112

STAGES = ("s1", "s2", "s3", "s4", "s5", "s6")
DEBUG = ()
SERIAL_PREPASS = True
FFN0_PHASES = 4
FFN0_SUB = 9
FFN0_MUTE = ""
ATTN_PH = 99
ATTN_SUB = 9


def mk(meth, *args, **kw):
    f = lambda e: getattr(e, meth)(*args, **kw)
    f.meth = meth
    f.args = args
    f.kw = kw
    return f


def _out_free(fn):
    meth = getattr(fn, "meth", None)
    if meth is None:
        return 0
    if meth == "engine_nop":
        return None
    out = fn.kw.get("out")
    if out is None and fn.args:
        out = fn.args[0]
    try:
        fs = out.free_size
        return fs() if callable(fs) else fs
    except Exception:
        return 0


class Sem:
    __slots__ = ("h", "n")
    def __init__(self, h):
        self.h = h
        self.n = 0


class KB:
    ENGS = ("sync", "tensor", "vector", "scalar", "gpsimd")
    SERIAL = False

    def __init__(self, nc, es):
        self.nc = nc
        self.es = es
        self.streams = {e: [] for e in self.ENGS}
        self.all_sems = []
        self.spacer = None
        self._v_seen = {}
        self._v_prev_narrow = False
        self.n_spacers = 0

    def sem(self, name):
        s = Sem(self.es.enter_context(self.nc.semaphore(name)))
        self.all_sems.append(s)
        return s

    def sb(self, name, shape, dt):
        return self.es.enter_context(self.nc.sbuf_tensor(name, shape, dt))

    def ps(self, name, shape, dt=F32):
        return self.es.enter_context(self.nc.psum_tensor(name, shape, dt))

    def emit(self, eng, fn, wait=(), inc=None, inc_n=1):
        if self.SERIAL:
            w = [(s.h, s.n) for s in self.all_sems]
        else:
            w = [(s.h, v) for (s, v) in wait]
        if eng == "vector":
            will_wait = any(self._v_seen.get(h, -1) < v for h, v in w)
            if self._v_prev_narrow and not will_wait and self.spacer is not None:
                self.streams[eng].append((self.spacer(), [], None))
                self.n_spacers += 1
            for h, v in w:
                if self._v_seen.get(h, -1) < v:
                    self._v_seen[h] = v
            of = _out_free(fn)
            self._v_prev_narrow = of is not None and of < 128
        ii = None
        if inc is not None:
            inc.n += inc_n
            ii = (inc.h, inc_n)
        self.streams[eng].append((fn, w, ii))

    def sy(self, fn, **kw): self.emit("sync", fn, **kw)
    def tn(self, fn, **kw): self.emit("tensor", fn, **kw)
    def v(self, fn, **kw): self.emit("vector", fn, **kw)
    def sc(self, fn, **kw): self.emit("scalar", fn, **kw)
    def g(self, fn, **kw): self.emit("gpsimd", fn, **kw)

    def dma(self, q, out, in_, sem, wait=(), slow=False):
        kw = {"allow_slow_non_contiguous": True} if slow else {}
        self.emit(q, mk("dma_start", out=out, in_=in_, **kw), wait=wait,
                  inc=sem, inc_n=16)

    def fence_waits(self):
        return [(s, s.n) for s in self.all_sems]

    def replay(self, name, eng):
        seen = {}
        for fn, waits, inc in self.streams[name]:
            for h, v in waits:
                if seen.get(h, -1) < v:
                    eng.wait_ge(h, v)
                    seen[h] = v
            inst = fn(eng)
            if inc is not None:
                inst.then_inc(*inc)


def xsl(c, t0, n):
    return slice(c * T + t0, c * T + t0 + n)


def build_program(stages, debug=()):
    nc = bass.Bass("TRN2", target_bir_lowering=False)
    dbg_ap = {}
    def dbg(name, shape, dt=F32):
        dbg_ap[name] = nc.dram_tensor("dbg_" + name, list(shape), dt,
                                      kind="ExternalOutput").ap()
        return dbg_ap[name]

    def dram_in(name, shape, dt=F32):
        return nc.dram_tensor(name, list(shape), dt, kind="ExternalInput").ap()

    x_in = dram_in("x", (B, D, S))
    out_d = nc.dram_tensor("out", [B, D, S], F32, kind="ExternalOutput").ap()

    w_gate = dram_in("L0_ffn_w_gate", (IF, D))
    w_up = dram_in("L0_ffn_w_up", (IF, D))
    w_down = dram_in("L0_ffn_w_down", (D, IF))
    clip_x_d = dram_in("L0_ffn_clip_x", (1,))
    clip_dn_d = dram_in("L0_ffn_clip_down", (1,))
    ln_g_d = dram_in("L0_attn_ln_g", (D,))
    ln_b_d = dram_in("L0_attn_ln_b", (D,))
    in_w_d = dram_in("L0_attn_in_w", (2 * DI, D))
    conv_w_d = dram_in("L0_attn_conv_w", (DI, 1, 4))
    conv_b_d = dram_in("L0_attn_conv_b", (DI,))
    xproj_d = dram_in("L0_attn_xproj_w", (33, DI))
    dt_w_d = dram_in("L0_attn_dt_w", (DI, 1))
    dt_b_d = dram_in("L0_attn_dt_b", (DI,))
    A_log_d = dram_in("L0_attn_A_log", (DS,))
    D_d = dram_in("L0_attn_D", (DI,))
    out_w_d = dram_in("L0_attn_out_w", (D, DI))
    kvd_d = dram_in("L1_attn_w_kv_down", (D, 128))
    kvu_d = dram_in("L1_attn_w_kv_up", (128, 2 * D))
    wq_d = dram_in("L1_attn_w_q", (D, D))
    wout_d = dram_in("L1_attn_w_out", (D, D))
    w_ffn1 = dram_in("L1_ffn_weight", (D, D))
    ident_bf = dram_in("c_ident_bf", (128, 128), BF16)
    ones_f = dram_in("c_ones_f", (128, 128))
    ident_f = dram_in("c_ident_f", (128, 128))
    ltneg = dram_in("c_ltneg", (128, 128))
    onehot = dram_in("c_onehot", (48, 33 * 128), BF16)

    h_wgT = nc.dram_tensor("h_wgT", [IFCH * 8, 128, 128], BF16)
    h_wuT = nc.dram_tensor("h_wuT", [IFCH * 8, 128, 128], BF16)
    h_wdT = nc.dram_tensor("h_wdT", [8 * IFCH, 128, 128], BF16)
    h_f1T = nc.dram_tensor("h_f1T", [8 * 8, 128, 128], BF16)
    h_inwT = nc.dram_tensor("h_inwT", [32 * 8, 128, 128], BF16)
    h_outwT = nc.dram_tensor("h_outwT", [8 * 16, 128, 128], BF16)
    h_xprojT = nc.dram_tensor("h_xprojT", [DI, 48], BF16)
    h_kvd = nc.dram_tensor("h_kvd", [D, 128], BF16)
    h_kvu = nc.dram_tensor("h_kvu", [128, 2 * D], BF16)
    h_wq = nc.dram_tensor("h_wq", [D, D], BF16)
    h_wout = nc.dram_tensor("h_wout", [D, D], BF16)
    h_hpre = nc.dram_tensor("h_hpre", [IFCH, 128, T], F32)
    h_xi = nc.dram_tensor("h_xi", [DICH, 128, T], BF16)
    h_z = nc.dram_tensor("h_z", [DICH, 128, T], BF16)
    cc_in = nc.dram_tensor("cc_in", [1, 1], F32)
    cc_out = nc.dram_tensor("cc_out", [1, 1], F32)

    es = contextlib.ExitStack()
    kb = KB(nc, es)

    xres = kb.sb("xres", [128, DCH * T], F32)
    cst = kb.sb("cst", [128, 192], F32)
    junkcol = kb.sb("junkcol", [128, 1], F32)
    kb.spacer = lambda: mk("tensor_scalar", out=junkcol[:, 0:1],
                           in0=junkcol[:, 0:1], scalar1=1.0, scalar2=None,
                           op0=OP.mult)
    ident_bf_sb = kb.sb("ident_bf_sb", [128, 128], BF16)
    ones_f_sb = kb.sb("ones_f_sb", [128, 128], F32)
    ident_f_sb = kb.sb("ident_f_sb", [128, 128], F32)
    ltneg_sb = kb.sb("ltneg_sb", [128, 128], F32)
    sq0 = kb.sb("sq0", [128, NT], F32)
    sq1 = kb.sb("sq1", [128, NT], F32)
    mu_b = kb.sb("mu_b", [128, NT], F32)
    m2_b = kb.sb("m2_b", [128, NT], F32)
    r_b = kb.sb("r_b", [128, NT], F32)
    mr_b = kb.sb("mr_b", [128, NT], F32)
    ex0 = kb.sb("ex0", [128, NT], F32)
    ex1 = kb.sb("ex1", [128, NT], F32)
    ex2 = kb.sb("ex2", [128, NT + 64], F32)
    wqi = kb.sb("wqi", [128, 1024], I32)
    POOL = kb.sb("POOL", [128, POOLW], BF16)
    col = {}
    for cn in ("acc", "tmp", "scale", "winv", "evac", "amax", "hmax",
               "sx", "sxi", "sh", "shi", "qn", "negM", "kmax", "lcol",
               "linv"):
        col[cn] = kb.sb("col_" + cn, [128, 2], F32)
    rowb = kb.sb("rowb", [1, 128], F32)
    colb = kb.sb("colb", [128, 256], F32)
    sc11 = kb.sb("sc11", [1, 8], F32)
    state_sb = kb.sb("state_sb", [128, 256], F32)
    wsg = kb.sb("wsg", [128, IFCH], F32)
    wsu = kb.sb("wsu", [128, IFCH], F32)
    wsd = kb.sb("wsd", [128, 8], F32)
    gscol = kb.sb("gscol", [128, IFCH], F32)
    uscol = kb.sb("uscol", [128, IFCH], F32)
    dscol = kb.sb("dscol", [128, 8], F32)

    def pb(off, w):
        return POOL[:, off:off + w]

    def pb3(off, ntiles):
        return POOL[:, off:off + ntiles * 128].rearrange(
            "p (t j) -> p t j", j=128)

    def thbm(h, t0, ntiles):
        return bass.AP(h, t0 * 16384, [[128, 128], [16384, ntiles], [1, 128]])

    C_ONE, C_EPS, C_I8, C_NEG1 = 0, 1, 2, 3
    CLN_G, CLN_B = 8, 16
    CCONV_B, CDT_B, CD_D, CDT_W = 24, 40, 56, 72
    CCONV_W = 96
    CA = 160

    PS_A = kb.ps("PS_A", [128, 2048])
    PS_B = kb.ps("PS_B", [128, 1024])
    PS_C = kb.ps("PS_C", [128, 512])
    PS_T = kb.ps("PS_T", [128, 1024], BF16)

    s_ld = kb.sem("s_ld")
    s_gld = kb.sem("s_gld")
    s_mm = kb.sem("s_mm")
    s_v = kb.sem("s_v")
    s_s = kb.sem("s_s")
    s_st = kb.sem("s_st")
    s_free = kb.sem("s_free")
    s_cc = kb.sem("s_cc")
    sem_nums = [s.h.num for s in kb.all_sems]
    clear_range = range(min(sem_nums), max(sem_nums) + 1)

    with nc.Block() as blk0:
        @blk0.gpsimd
        def _(g):
            g.dma_reset(clear_range)
            g.sem_clear(clear_range)

    with es, nc.Block() as block:
        kb.dma("sync", ident_bf_sb[:, :], ident_bf[:, :], s_ld)
        kb.dma("sync", ones_f_sb[:, :], ones_f[:, :], s_ld)
        kb.dma("sync", ident_f_sb[:, :], ident_f[:, :], s_ld)
        kb.dma("sync", ltneg_sb[:, :], ltneg[:, :], s_ld)
        for idx, val in ((C_ONE, 1.0), (C_EPS, EPS), (C_I8, 0.125),
                         (C_NEG1, -1.0)):
            kb.v(mk("memset", cst[:, idx:idx + 1], val), inc=s_v)
        kb.dma("sync", cst[:, CLN_G:CLN_G + 8],
               ln_g_d.rearrange("(c p) -> p c", p=128), s_ld, slow=True)
        kb.dma("sync", cst[:, CLN_B:CLN_B + 8],
               ln_b_d.rearrange("(c p) -> p c", p=128), s_ld, slow=True)
        kb.dma("sync", cst[:, CCONV_B:CCONV_B + 16],
               conv_b_d.rearrange("(c p) -> p c", p=128), s_ld, slow=True)
        kb.dma("sync", cst[:, CDT_B:CDT_B + 16],
               dt_b_d.rearrange("(c p) -> p c", p=128), s_ld, slow=True)
        kb.dma("sync", cst[:, CD_D:CD_D + 16],
               D_d.rearrange("(c p) -> p c", p=128), s_ld, slow=True)
        kb.dma("sync", cst[:, CDT_W:CDT_W + 16],
               dt_w_d[:, 0].rearrange("(c p) -> p c", p=128), s_ld, slow=True)
        for k in range(4):
            kb.dma("sync", cst[:, CCONV_W + k * 16:CCONV_W + k * 16 + 16],
                   conv_w_d[:, 0, k].rearrange("(c p) -> p c", p=128), s_ld, slow=True)
        kb.dma("sync", sc11[0:1, 0:1], clip_x_d[None, 0:1], s_ld)
        kb.dma("sync", sc11[0:1, 1:2], clip_dn_d[None, 0:1], s_ld)
        kb.dma("sync", rowb[0:1, 0:16], A_log_d[None, :], s_ld)
        lds0 = s_ld.n
        kb.sc(mk("activation", out=rowb[0:1, 0:16], in_=rowb[0:1, 0:16],
                 func=AF.Exp), wait=[(s_ld, lds0)], inc=s_s)
        kb.v(mk("tensor_scalar", out=rowb[0:1, 0:16], in0=rowb[0:1, 0:16],
                scalar1=-1.0, scalar2=None, op0=OP.mult),
             wait=[(s_s, s_s.n)], inc=s_v)
        kb.sc(mk("activation", out=sc11[0:1, 2:3], in_=sc11[0:1, 0:1],
                 func=AF.Sigmoid), wait=[(s_ld, lds0)], inc=s_s)
        kb.sc(mk("activation", out=sc11[0:1, 3:4], in_=sc11[0:1, 1:2],
                 func=AF.Sigmoid), inc=s_s)
        kb.tn(mk("matmul", PS_C[:, 0:2], ones_f_sb[0:1, 0:128],
                 sc11[0:1, 2:4], start=True, stop=True),
              wait=[(s_s, s_s.n)], inc=s_mm)
        kb.v(mk("tensor_copy", out=cst[:, 4:6], in_=PS_C[:, 0:2]),
             wait=[(s_mm, s_mm.n)], inc=s_v)
        kb.tn(mk("matmul", PS_B[:, 0:16], ones_f_sb[0:1, 0:128],
                 rowb[0:1, 0:16], start=True, stop=True),
              wait=[(s_v, s_v.n), (s_ld, lds0)], inc=s_mm)
        kb.v(mk("tensor_copy", out=cst[:, CA:CA + 16], in_=PS_B[:, 0:16]),
             wait=[(s_mm, s_mm.n)], inc=s_v)

        kb.SERIAL = SERIAL_PREPASS
        def prep_load_bulk(src, R, C):
            for rc in range(R // 128):
                kb.dma("sync", xres[:, rc * C:(rc + 1) * C],
                       src[rc * 128:(rc + 1) * 128, :], s_ld,
                       wait=[(s_v, s_v.n), (s_st, s_st.n)] if rc == 0 else [])
            return s_ld.n

        def cast_int_bf(sl_f32, C, dst_off):
            for sub in range(0, C, 1024):
                w = min(1024, C - sub)
                kb.v(mk("tensor_copy", out=wqi[:, 0:w],
                        in_=sl_f32[:, sub:sub + w]), inc=s_v)
                kb.v(mk("tensor_copy", out=pb(dst_off + sub, w),
                        in_=wqi[:, 0:w]),
                     wait=[(s_mm, s_mm.n)], inc=s_v)

        def transpose_store(src_off, C, dst, rc):
            CC = C // 128
            st_off = 2816
            for w0 in range(0, CC, 8):
                nw = min(8, CC - w0)
                for j in range(nw):
                    cc = w0 + j
                    w = [(s_v, s_v.n), (s_st, s_st.n)] if j == 0 else []
                    kb.tn(mk("transpose", PS_T[:, j * 128:(j + 1) * 128],
                             pb(src_off + cc * 128, 128),
                             ident_bf_sb[:, :]), wait=w, inc=s_mm)
                kb.v(mk("tensor_copy",
                        out=pb(st_off + w0 * 128, nw * 128),
                        in_=PS_T[:, 0:nw * 128]),
                     wait=[(s_mm, s_mm.n)], inc=s_v)
            kb.dma("sync", thbm(dst, rc * CC, ((rc + 1) * CC) - (rc * CC)),
                   pb3(st_off, CC),
                   s_st, wait=[(s_v, s_v.n)])

        def prep_rowquant(src, dst, R, C, ws_sb):
            ldn = prep_load_bulk(src, R, C)
            for rc in range(R // 128):
                sl = xres[:, rc * C:(rc + 1) * C]
                kb.v(mk("tensor_reduce", out=col["tmp"][:, 0:1], in_=sl,
                        axis=AX.X, op=OP.max, apply_absolute_value=True),
                     wait=[(s_ld, ldn)], inc=s_v)
                kb.v(mk("tensor_scalar", out=ws_sb[:, rc:rc + 1],
                        in0=col["tmp"][:, 0:1], scalar1=1.0 / QMAX,
                        scalar2=None, op0=OP.mult), inc=s_v)
                kb.v(mk("reciprocal", out=col["winv"][:, 0:1],
                        in_=ws_sb[:, rc:rc + 1]), inc=s_v)
                kb.v(mk("tensor_scalar", out=sl, in0=sl,
                        scalar1=col["winv"][:, 0:1], scalar2=-128.0,
                        op0=OP.mult, op1=OP.max), inc=s_v)
                kb.v(mk("tensor_scalar", out=sl, in0=sl, scalar1=QMAX,
                        scalar2=None, op0=OP.min),
                     wait=[(s_st, s_st.n)], inc=s_v)
                cast_int_bf(sl, C, 0)
                transpose_store(0, C, dst, rc)

        def prep_cast_transposed(src, dst, R, C):
            ldn = prep_load_bulk(src, R, C)
            for rc in range(R // 128):
                kb.v(mk("tensor_copy", out=pb(0, C),
                        in_=xres[:, rc * C:(rc + 1) * C]),
                     wait=[(s_ld, ldn), (s_st, s_st.n), (s_mm, s_mm.n)],
                     inc=s_v)
                transpose_store(0, C, dst, rc)

        def prep_cast_natural(src, dst, R, C):
            P = min(R, 128)
            if R >= 128:
                ldn = prep_load_bulk(src, R, C)
            else:
                kb.dma("sync", xres[0:R, 0:C], src[:, :], s_ld,
                       wait=[(s_v, s_v.n), (s_st, s_st.n)])
                ldn = s_ld.n
            for rc in range(max(R // 128, 1)):
                kb.v(mk("tensor_copy", out=POOL[0:P, 0:C],
                        in_=xres[0:P, rc * C:(rc + 1) * C]),
                     wait=[(s_ld, ldn), (s_st, s_st.n)], inc=s_v)
                kb.dma("sync", dst.ap()[rc * 128:rc * 128 + P, :],
                       POOL[0:P, 0:C], s_st, wait=[(s_v, s_v.n)])

        ldn = prep_load_bulk(w_ffn1, D, D)
        for rc in range(8):
            kb.v(mk("tensor_reduce", out=colb[:, rc:rc + 1],
                    in_=xres[:, rc * D:(rc + 1) * D], axis=AX.X, op=OP.add,
                    apply_absolute_value=True), wait=[(s_ld, ldn)], inc=s_v)
        kb.v(mk("tensor_reduce", out=col["acc"][:, 0:1], in_=colb[:, 0:8],
                axis=AX.X, op=OP.add), inc=s_v)
        kb.tn(mk("matmul", PS_C[:, 0:1], ones_f_sb[:, :], col["acc"][:, 0:1],
                 start=True, stop=True),
              wait=[(s_v, s_v.n), (s_ld, lds0)], inc=s_mm)
        kb.v(mk("tensor_scalar", out=col["scale"][:, 0:1], in0=PS_C[:, 0:1],
                scalar1=1.0 / (D * D), scalar2=EPS, op0=OP.mult, op1=OP.max),
             wait=[(s_mm, s_mm.n)], inc=s_v)
        kb.v(mk("reciprocal", out=col["winv"][:, 0:1],
                in_=col["scale"][:, 0:1]), inc=s_v)
        kb.v(mk("tensor_scalar", out=col["evac"][:, 0:1],
                in0=col["scale"][:, 0:1], scalar1=0.5, scalar2=None,
                op0=OP.mult), inc=s_v)
        for rc in range(8):
            sl = xres[:, rc * D:(rc + 1) * D]
            kb.v(mk("tensor_scalar", out=sl, in0=sl,
                    scalar1=col["winv"][:, 0:1], scalar2=-1.0,
                    op0=OP.mult, op1=OP.max), inc=s_v)
            kb.v(mk("tensor_scalar", out=sl, in0=sl, scalar1=1.0,
                    scalar2=None, op0=OP.min),
                 wait=[(s_st, s_st.n)], inc=s_v)
            cast_int_bf(sl, D, 0)
            transpose_store(0, D, h_f1T, rc)
        prep_rowquant(w_gate, h_wgT, IF, D, wsg)
        prep_rowquant(w_up, h_wuT, IF, D, wsu)
        prep_rowquant(w_down, h_wdT, D, IF, wsd)
        prep_cast_transposed(in_w_d, h_inwT, 2 * DI, D)
        prep_cast_transposed(out_w_d, h_outwT, D, DI)
        prep_cast_natural(kvd_d, h_kvd, D, 128)
        prep_cast_natural(kvu_d, h_kvu, 128, 2 * D)
        prep_cast_natural(wq_d, h_wq, D, D)
        prep_cast_natural(wout_d, h_wout, D, D)
        kb.dma("sync", xres[0:33, 0:DI], xproj_d[:, :], s_ld,
               wait=[(s_v, s_v.n), (s_st, s_st.n)])
        ldx = s_ld.n
        kb.v(mk("memset", POOL[0:48, 0:DI], 0.0),
             wait=[(s_st, s_st.n)], inc=s_v)
        kb.v(mk("tensor_copy", out=POOL[0:33, 0:DI], in_=xres[0:33, 0:DI]),
             wait=[(s_ld, ldx)], inc=s_v)
        for k in range(DICH):
            kb.tn(mk("transpose", PS_T[:, 0:48],
                     POOL[0:48, k * 128:(k + 1) * 128],
                     ident_bf_sb[0:48, 0:48]),
                  wait=[(s_v, s_v.n), (s_st, s_st.n)], inc=s_mm)
            kb.v(mk("tensor_copy", out=pb(2816, 48), in_=PS_T[:, 0:48]),
                 wait=[(s_mm, s_mm.n)], inc=s_v)
            kb.dma("sync", h_xprojT.ap()[k * 128:(k + 1) * 128, :],
                   pb(2816, 48), s_st, wait=[(s_v, s_v.n)])

        kb.SERIAL = False
        kb.v(mk("memset", state_sb[:, :], 0.0), inc=s_v)
        for b in range(B):
            for c in range(DCH):
                kb.dma("sync", xres[:, xsl(c, b * S, S)],
                       x_in[b, c * 128:(c + 1) * 128, :], s_ld,
                       wait=(kb.fence_waits() if (b == 0 and c == 0) else []))
        lds_done = s_ld.n

        def emit_stats(t0, n):
            base = s_mm.n
            for c in range(DCH):
                sq = (sq0, sq1)[c % 2]
                w = [(s_mm, base + (c - 1))] if c >= 2 else [(s_mm, base)]
                if c == 0:
                    w.append((s_v, s_v.n))
                    w.append((s_st, s_st.n))
                kb.sc(mk("activation", out=sq[:, :n],
                         in_=xres[:, xsl(c, t0, n)], func=AF.Square),
                      wait=w + [(s_ld, lds_done)], inc=s_s)
                mmw = [(s_s, s_s.n)]
                if c == 0:
                    mmw += [(s_v, s_v.n), (s_free, s_free.n)]
                kb.tn(mk("matmul", PS_B[:, 0:n], ones_f_sb[:, :],
                         xres[:, xsl(c, t0, n)],
                         start=(c == 0), stop=(c == DCH - 1)),
                      wait=mmw, inc=(s_mm if c == DCH - 1 else None))
                kb.tn(mk("matmul", PS_B[:, 512:512 + n], ones_f_sb[:, :],
                         sq[:, :n], start=(c == 0), stop=(c == DCH - 1)),
                      inc=s_mm)
            return s_mm.n

        def emit_mu_r(n, stats_done):
            kb.v(mk("tensor_scalar", out=mu_b[:, :n], in0=PS_B[:, 0:n],
                    scalar1=1.0 / D, scalar2=None, op0=OP.mult),
                 wait=[(s_mm, stats_done)], inc=s_v)
            kb.sc(mk("activation", out=m2_b[:, :n], in_=mu_b[:, :n],
                     func=AF.Square), wait=[(s_v, s_v.n)], inc=s_s)
            kb.v(mk("scalar_tensor_tensor", out=m2_b[:, :n],
                    in0=PS_B[:, 512:512 + n], scalar=1.0 / D, in1=m2_b[:, :n],
                    op0=OP.mult, op1=OP.subtract),
                 wait=[(s_s, s_s.n)], inc=s_v)
            kb.sc(mk("activation", out=m2_b[:, :n], in_=m2_b[:, :n],
                     func=AF.Sqrt, bias=cst[:, C_EPS:C_EPS + 1]),
                  wait=[(s_v, s_v.n)], inc=s_s)
            kb.v(mk("reciprocal", out=r_b[:, :n], in_=m2_b[:, :n]),
                 wait=[(s_s, s_s.n)], inc=s_v)
            kb.v(mk("tensor_mul", out=mr_b[:, :n], in0=mu_b[:, :n],
                    in1=r_b[:, :n]), inc=s_v)
            return s_v.n

        def emit_xn_bf16(t0, n, out_off, ostride):
            sv = s_v.n
            for c in range(DCH):
                kb.v(mk("tensor_mul", out=ex0[:, :n],
                        in0=xres[:, xsl(c, t0, n)], in1=r_b[:, :n]),
                     wait=[(s_v, sv)], inc=s_v)
                kb.v(mk("tensor_sub", out=pb(out_off + c * ostride, n),
                        in0=ex0[:, :n], in1=mr_b[:, :n]), inc=s_v)
            return s_v.n

        def crosspart_max(src_col):
            kb.tn(mk("matmul", PS_C[0:1, 0:128], src_col[:, 0:1],
                     ident_f_sb[:, :], start=True, stop=True),
                  wait=[(s_v, s_v.n), (s_free, s_free.n)], inc=s_mm)
            kb.v(mk("tensor_copy", out=rowb[0:1, :], in_=PS_C[0:1, 0:128]),
                 wait=[(s_mm, s_mm.n)], inc=s_v)
            kb.v(mk("reduce_max", out=sc11[0:1, 0:1], in_=rowb[0:1, :],
                    axis=AX.X), inc=s_v)

        def allreduce_max():
            kb.g(mk("dma_start", out=cc_in.ap(), in_=sc11[0:1, 0:1]),
                 wait=[(s_v, s_v.n)], inc=s_gld, inc_n=16)
            kb.g(mk("collective_compute", "AllReduce", OP.max,
                    replica_groups=[list(range(NCORES))],
                    ins=[cc_in.ap().opt()], outs=[cc_out.ap().opt()]),
                 wait=[(s_gld, s_gld.n)], inc=s_cc)
            kb.g(mk("dma_start", out=sc11[0:1, 0:1], in_=cc_out.ap()),
                 wait=[(s_cc, s_cc.n)], inc=s_gld, inc_n=16)

        def bcast11(dst_col):
            kb.tn(mk("matmul", PS_C[:, 0:1], ones_f_sb[0:1, 0:128],
                     sc11[0:1, 0:1], start=True, stop=True),
                  wait=[(s_gld, s_gld.n), (s_v, s_v.n), (s_free, s_free.n)],
                  inc=s_mm)
            kb.v(mk("tensor_copy", out=dst_col[:, 0:1], in_=PS_C[:, 0:1]),
                 wait=[(s_mm, s_mm.n)], inc=s_v)

        def emit_ffn0(tag):
            XQ, WS = 0, 16384
            PH = FFN0_PHASES
            for tt in range(NTT):
                t0 = tt * NT
                sd = emit_stats(t0, NT)
                emit_mu_r(NT, sd)
                sv = s_v.n
                for c in range(DCH):
                    kb.v(mk("tensor_mul", out=ex0[:, :],
                            in0=xres[:, xsl(c, t0, NT)], in1=r_b[:, :]),
                         wait=[(s_v, sv)], inc=s_v)
                    kb.v(mk("tensor_sub", out=ex0[:, :], in0=ex0[:, :],
                            in1=mr_b[:, :]), inc=s_v)
                    kb.v(mk("tensor_reduce",
                            out=colb[:, tt * 8 + c:tt * 8 + c + 1],
                            in_=ex0[:, :], axis=AX.X, op=OP.max,
                            apply_absolute_value=True), inc=s_v)
            if PH < 2:
                return
            kb.v(mk("tensor_reduce", out=col["amax"][:, 0:1],
                    in_=colb[:, 0:64], axis=AX.X, op=OP.max), inc=s_v)
            if "am" in debug and tag == "s1":
                kb.dma("sync", dbg("amax", (128, 1)), col["amax"][:, 0:1],
                       s_st, wait=[(s_v, s_v.n)])
            crosspart_max(col["amax"])
            if "am" in debug and tag == "s1":
                kb.dma("sync", dbg("gmax_pre", (1, 4)), sc11[0:1, 0:4],
                       s_st, wait=[(s_v, s_v.n)])
            allreduce_max()
            if "am" in debug and tag == "s1":
                kb.v(mk("tensor_copy", out=rowb[0:1, 8:12],
                        in_=sc11[0:1, 0:4]), wait=[(s_gld, s_gld.n)],
                     inc=s_v)
                kb.dma("sync", dbg("gmax_post", (1, 4)), rowb[0:1, 8:12],
                       s_st, wait=[(s_v, s_v.n)])
            bcast11(col["sx"])
            kb.v(mk("tensor_mul", out=col["sx"][:, 0:1],
                    in0=col["sx"][:, 0:1], in1=cst[:, 4:5]), inc=s_v)
            kb.v(mk("tensor_scalar", out=col["sx"][:, 0:1],
                    in0=col["sx"][:, 0:1], scalar1=1.0 / QMAX,
                    scalar2=None, op0=OP.mult), inc=s_v)
            kb.sc(mk("activation", out=col["sxi"][:, 0:1],
                     in_=col["sx"][:, 0:1], func=AF.Ln),
                  wait=[(s_v, s_v.n)], inc=s_s)
            kb.sc(mk("activation", out=col["sxi"][:, 0:1],
                     in_=col["sxi"][:, 0:1], func=AF.Exp,
                     scale=cst[:, C_NEG1:C_NEG1 + 1]), inc=s_s)
            kb.v(mk("engine_nop"), wait=[(s_s, s_s.n)], inc=s_v)
            if "q" in debug and tag == "s1":
                kb.dma("sync", dbg("sxi_early", (128, 1)),
                       col["sxi"][:, 0:1], s_st, wait=[(s_v, s_v.n)])
            for i in range(IFCH):
                kb.v(mk("tensor_mul", out=gscol[:, i:i + 1],
                        in0=wsg[:, i:i + 1], in1=col["sx"][:, 0:1]), inc=s_v)
                kb.v(mk("tensor_mul", out=uscol[:, i:i + 1],
                        in0=wsu[:, i:i + 1], in1=col["sx"][:, 0:1]), inc=s_v)
            hslot = [0]
            sxv = s_v.n
            if "q" in debug and tag == "s1":
                kb.dma("sync", dbg("cst8", (128, 8)), cst[:, 0:8], s_st,
                       wait=[(s_v, s_v.n)])
                kb.dma("sync", dbg("sx", (128, 1)), col["sx"][:, 0:1], s_st,
                       wait=[(s_v, s_v.n)])
                kb.dma("sync", dbg("sxi", (128, 1)), col["sxi"][:, 0:1], s_st)
                kb.dma("sync", dbg("wsg", (128, IFCH)), wsg[:, :], s_st)
                kb.dma("sync", dbg("gscol", (128, IFCH)), gscol[:, :], s_st)
            if PH < 3:
                return
            for half in range(2):
                ht0 = half * (T // 2)
                for tt4 in range(4):
                    t0 = ht0 + tt4 * NT
                    sd = emit_stats(t0, NT)
                    emit_mu_r(NT, sd)
                    kb.v(mk("tensor_scalar", out=ex1[:, :], in0=r_b[:, :],
                            scalar1=col["sxi"][:, 0:1], scalar2=None,
                            op0=OP.mult), wait=[(s_v, sxv)], inc=s_v)
                    kb.v(mk("tensor_mul", out=ex2[:, 0:NT], in0=mu_b[:, :],
                            in1=ex1[:, :]), inc=s_v)
                    sv = s_v.n
                    if "q2" in debug and tag == "s1" and half == 0 and tt4 == 0:
                        kb.dma("sync", dbg("mu", (128, NT)), mu_b[:, :], s_st,
                               wait=[(s_v, s_v.n)])
                        kb.dma("sync", dbg("rb", (128, NT)), r_b[:, :], s_st)
                        kb.dma("sync", dbg("ex1", (128, NT)), ex1[:, :], s_st)
                        kb.dma("sync", dbg("ex2", (128, NT)), ex2[:, 0:NT],
                               s_st)
                    for c in range(DCH):
                        kb.v(mk("tensor_mul", out=ex0[:, :],
                                in0=xres[:, xsl(c, t0, NT)], in1=ex1[:, :]),
                             wait=[(s_v, sv)], inc=s_v)
                        kb.v(mk("tensor_sub", out=ex0[:, :], in0=ex0[:, :],
                                in1=ex2[:, 0:NT]), inc=s_v)
                        kb.v(mk("tensor_scalar", out=ex0[:, :], in0=ex0[:, :],
                                scalar1=-128.0, scalar2=QMAX,
                                op0=OP.max, op1=OP.min), inc=s_v)
                        kb.v(mk("tensor_copy", out=wqi[:, 0:NT],
                                in_=ex0[:, :]), inc=s_v)
                        kb.v(mk("tensor_copy",
                                out=pb(XQ + c * 2048 + tt4 * NT, NT),
                                in_=wqi[:, 0:NT]), inc=s_v)
                xqv = s_v.n
                if "q" in debug and tag == "s1" and half == 0:
                    kb.dma("sync", dbg("xq%d" % half, (128, 2048), BF16),
                           pb(XQ, 2048), s_st, wait=[(s_v, xqv)])
                if FFN0_SUB < 1:
                    continue
                for ig in range(min((IFCH + 3) // 4, FFN0_SUB)):
                    i0 = ig * 4
                    ni = min(4, IFCH - i0)
                    for i in range(ni):
                        for t in range(8):
                            kb.dma("sync", pb(WS + i * 1024 + t * 128, 128),
                                   h_wgT.ap()[(i0 + i) * 8 + t, :, :], s_ld,
                                   wait=[(s_mm, s_mm.n)]
                                   if (i == 0 and t == 0) else [])
                            kb.dma("sync",
                                   pb(WS + 4096 + i * 1024 + t * 128, 128),
                                   h_wuT.ap()[(i0 + i) * 8 + t, :, :], s_ld)
                    ldwn = s_ld.n
                    if FFN0_MUTE == "loadsonly":
                        continue
                    for tt4 in range(4):
                        if FFN0_MUTE in ("gateonly", "up1") and tt4 > 0:
                            continue
                        for i in range(ni):
                            for k in range(DCH):
                                w = []
                                if i == 0 and k == 0:
                                    w = [(s_ld, ldwn), (s_v, s_v.n),
                                         (s_s, s_s.n), (s_free, s_free.n)]
                                kb.tn(mk("matmul",
                                         PS_A[:, i * 512:(i + 1) * 512],
                                         pb(WS + i * 1024 + k * 128, 128),
                                         pb(XQ + k * 2048 + tt4 * NT, NT),
                                         start=(k == 0), stop=(k == DCH - 1)),
                                      wait=w,
                                      inc=(s_mm if k == DCH - 1 else None))
                        mmg = s_mm.n
                        silu_bufs = (ex0, ex1, mu_b, m2_b)
                        for i in range(ni):
                            if FFN0_MUTE == "nosilu":
                                kb.sc(mk("activation",
                                         out=silu_bufs[i][:, 0:NT],
                                         in_=PS_A[:, i * 512:(i + 1) * 512],
                                         func=AF.Silu),
                                      wait=[(s_mm, mmg), (s_v, s_v.n)],
                                      inc=s_s)
                                continue
                            kb.sc(mk("activation", out=silu_bufs[i][:, 0:NT],
                                     in_=PS_A[:, i * 512:(i + 1) * 512],
                                     func=AF.Silu,
                                     scale=gscol[:, i0 + i:i0 + i + 1]),
                                  wait=[(s_mm, mmg), (s_v, s_v.n)], inc=s_s)
                        sg_done = s_s.n
                        if FFN0_MUTE in ("gateonly", "gate4"):
                            continue
                        for i in range(ni):
                            for k in range(DCH):
                                w = [(s_s, sg_done)] if (i == 0 and k == 0) \
                                    else []
                                kb.tn(mk("matmul",
                                         PS_A[:, i * 512:(i + 1) * 512],
                                         pb(WS + 4096 + i * 1024 + k * 128,
                                            128),
                                         pb(XQ + k * 2048 + tt4 * NT, NT),
                                         start=(k == 0), stop=(k == DCH - 1)),
                                      wait=w,
                                      inc=(s_mm if k == DCH - 1 else None))
                        mmu = s_mm.n
                        for i in range(ni):
                            kb.v(mk("scalar_tensor_tensor", out=sq1[:, :],
                                    in0=PS_A[:, i * 512:(i + 1) * 512],
                                    scalar=uscol[:, i0 + i:i0 + i + 1],
                                    in1=silu_bufs[i][:, 0:NT],
                                    op0=OP.mult, op1=OP.mult),
                                 wait=[(s_mm, mmu), (s_st, s_st.n)], inc=s_v)
                            kb.v(mk("tensor_reduce",
                                    out=colb[:, 64 + hslot[0]:
                                             64 + hslot[0] + 1],
                                    in_=sq1[:, :], axis=AX.X, op=OP.max,
                                    apply_absolute_value=True), inc=s_v)
                            hslot[0] += 1
                            if FFN0_MUTE != "nodma":
                                kb.dma("sync",
                                       h_hpre.ap()[i0 + i, :,
                                                   ht0 + tt4 * NT:
                                                   ht0 + tt4 * NT + NT],
                                       sq1[:, :], s_st,
                                       wait=[(s_v, s_v.n)])
            if PH < 4:
                return
            assert hslot[0] <= 192
            kb.v(mk("tensor_reduce", out=col["hmax"][:, 0:1],
                    in_=colb[:, 64:64 + hslot[0]], axis=AX.X, op=OP.max,
                    apply_absolute_value=True), inc=s_v)
            crosspart_max(col["hmax"])
            allreduce_max()
            bcast11(col["sh"])
            kb.v(mk("tensor_mul", out=col["sh"][:, 0:1],
                    in0=col["sh"][:, 0:1], in1=cst[:, 5:6]), inc=s_v)
            kb.v(mk("tensor_scalar", out=col["sh"][:, 0:1],
                    in0=col["sh"][:, 0:1], scalar1=1.0 / QMAX,
                    scalar2=None, op0=OP.mult), inc=s_v)
            kb.sc(mk("activation", out=col["shi"][:, 0:1],
                     in_=col["sh"][:, 0:1], func=AF.Ln),
                  wait=[(s_v, s_v.n)], inc=s_s)
            kb.sc(mk("activation", out=col["shi"][:, 0:1],
                     in_=col["shi"][:, 0:1], func=AF.Exp,
                     scale=cst[:, C_NEG1:C_NEG1 + 1]), inc=s_s)
            kb.v(mk("engine_nop"), wait=[(s_s, s_s.n)], inc=s_v)
            for d in range(8):
                kb.v(mk("scalar_tensor_tensor", out=dscol[:, d:d + 1],
                        in0=wsd[:, d:d + 1], scalar=0.5, in1=col["sh"][:, 0:1],
                        op0=OP.mult, op1=OP.mult), inc=s_v)
            if "hp" in debug and tag == "s1":
                kb.dma("sync", dbg("hpre", (IFCH, 128, T)), h_hpre.ap()[:, :, :],
                       s_st, wait=[(s_st, s_st.n)])
            HQ, WD = 0, 11264
            for tt in range(NTT):
                t0 = tt * NT
                for ik in range(IFCH):
                    sq = (sq0, sq1)[ik % 2]
                    kb.dma("sync", sq[:, :], h_hpre.ap()[ik, :, t0:t0 + NT],
                           s_ld, wait=[(s_v, s_v.n)])
                    ldh = s_ld.n
                    kb.v(mk("tensor_scalar", out=ex0[:, :], in0=sq[:, :],
                            scalar1=col["shi"][:, 0:1], scalar2=-128.0,
                            op0=OP.mult, op1=OP.max),
                         wait=[(s_ld, ldh)], inc=s_v)
                    kb.v(mk("tensor_scalar", out=ex0[:, :], in0=ex0[:, :],
                            scalar1=QMAX, scalar2=None, op0=OP.min), inc=s_v)
                    kb.v(mk("tensor_copy", out=wqi[:, 0:NT], in_=ex0[:, :]),
                         inc=s_v)
                    kb.v(mk("tensor_copy", out=pb(HQ + ik * NT, NT),
                            in_=wqi[:, 0:NT]), inc=s_v)
                hqv = s_v.n
                if "hq" in debug and tag == "s1" and tt in (0, 3):
                    kb.dma("sync", dbg("hq%d" % tt, (128, 11264), BF16),
                           pb(HQ, 11264), s_st, wait=[(s_v, hqv)])
                for dd in range(DCH):
                    kb.dma("sync",
                           pb3(WD + (dd % 2) * 2816, 22),
                           thbm(h_wdT, dd * IFCH, ((dd + 1) * IFCH) - (dd * IFCH)), s_ld,
                           wait=[(s_mm, s_mm.n)])
                    ldw = s_ld.n
                    bank = PS_C if dd % 2 == 0 else PS_B
                    for ik in range(IFCH):
                        w = []
                        if ik == 0:
                            w = [(s_ld, ldw), (s_v, hqv), (s_free, s_free.n)]
                        kb.tn(mk("matmul", bank[:, 0:NT],
                                 pb(WD + (dd % 2) * 2816 + ik * 128, 128),
                                 pb(HQ + ik * NT, NT),
                                 start=(ik == 0), stop=(ik == IFCH - 1)),
                              wait=w, inc=(s_mm if ik == IFCH - 1 else None))
                    kb.v(mk("scalar_tensor_tensor",
                            out=xres[:, xsl(dd, t0, NT)], in0=bank[:, 0:NT],
                            scalar=dscol[:, dd:dd + 1],
                            in1=xres[:, xsl(dd, t0, NT)],
                            op0=OP.mult, op1=OP.add),
                         wait=[(s_mm, s_mm.n)], inc=s_free)

        def emit_ffn1(tag):
            XN, WS = 0, 4096
            kb.dma("sync", pb3(WS, 64),
                   thbm(h_f1T, 0, 64), s_ld,
                   wait=[(s_st, s_st.n), (s_mm, s_mm.n)])
            wslab = s_ld.n
            for tt in range(NTT):
                t0 = tt * NT
                sd = emit_stats(t0, NT)
                emit_mu_r(NT, sd)
                lnv = emit_xn_bf16(t0, NT, XN, NT)
                for d in range(DCH):
                    bank = PS_C if d % 2 == 0 else PS_B
                    for k in range(DCH):
                        w = []
                        if k == 0:
                            w = [(s_ld, wslab), (s_v, lnv),
                                 (s_free, s_free.n)]
                        kb.tn(mk("matmul", bank[:, 0:NT],
                                 pb(WS + (d * 8 + k) * 128, 128),
                                 pb(XN + k * NT, NT),
                                 start=(k == 0), stop=(k == DCH - 1)),
                              wait=w, inc=(s_mm if k == DCH - 1 else None))
                    kb.v(mk("scalar_tensor_tensor",
                            out=xres[:, xsl(d, t0, NT)], in0=bank[:, 0:NT],
                            scalar=col["evac"][:, 0:1],
                            in1=xres[:, xsl(d, t0, NT)],
                            op0=OP.mult, op1=OP.add),
                         wait=[(s_mm, s_mm.n)], inc=s_free)

        def emit_mamba():
            INRES, INS, XN2, EV = 0, 17408, 19456, 23552
            NRES = 17
            for e in range(NRES):
                kb.dma("sync",
                       pb3(INRES + e * 1024, 8),
                       thbm(h_inwT, e * 8, ((e + 1) * 8) - (e * 8)), s_ld,
                       wait=(kb.fence_waits() if e == 0 else []))
            winw = s_ld.n
            for tt in range(NTT):
                t0 = tt * NT
                sd = emit_stats(t0, NT)
                emit_mu_r(NT, sd)
                kb.sc(mk("activation", out=ex0[:, :], in_=r_b[:, :],
                         func=AF.Square), wait=[(s_v, s_v.n)], inc=s_s)
                kb.v(mk("tensor_scalar", out=ex0[:, :], in0=ex0[:, :],
                        scalar1=-EPS, scalar2=1.0 + EPS,
                        op0=OP.mult, op1=OP.add),
                     wait=[(s_s, s_s.n)], inc=s_v)
                kb.sc(mk("activation", out=ex0[:, :], in_=ex0[:, :],
                         func=AF.Sqrt), wait=[(s_v, s_v.n)], inc=s_s)
                kb.v(mk("reciprocal", out=ex1[:, :], in_=ex0[:, :]),
                     wait=[(s_s, s_s.n)], inc=s_v)
                kb.v(mk("tensor_mul", out=r_b[:, :], in0=r_b[:, :],
                        in1=ex1[:, :]), inc=s_v)
                kb.v(mk("tensor_mul", out=mr_b[:, :], in0=mu_b[:, :],
                        in1=r_b[:, :]), inc=s_v)
                sv = s_v.n
                for c in range(DCH):
                    kb.v(mk("tensor_mul", out=ex0[:, :],
                            in0=xres[:, xsl(c, t0, NT)], in1=r_b[:, :]),
                         wait=[(s_v, sv)], inc=s_v)
                    kb.v(mk("tensor_sub", out=ex0[:, :], in0=ex0[:, :],
                            in1=mr_b[:, :]), inc=s_v)
                    kb.v(mk("tensor_scalar", out=pb(XN2 + c * NT, NT),
                            in0=ex0[:, :],
                            scalar1=cst[:, CLN_G + c:CLN_G + c + 1],
                            scalar2=cst[:, CLN_B + c:CLN_B + c + 1],
                            op0=OP.mult, op1=OP.add), inc=s_v)
                xnv = s_v.n
                for e in range(32):
                    if e < NRES:
                        woff = INRES + e * 1024
                        ldreq = winw
                    else:
                        woff = INS + (e % 2) * 1024
                        kb.dma("sync",
                               pb3(woff, 8),
                               thbm(h_inwT, e * 8, ((e + 1) * 8) - (e * 8)), s_ld,
                               wait=[(s_mm, s_mm.n)])
                        ldreq = s_ld.n
                    bank = PS_C if e % 2 == 0 else PS_B
                    for k in range(DCH):
                        w = []
                        if k == 0:
                            w = [(s_ld, ldreq), (s_v, xnv),
                                 (s_free, s_free.n)]
                        kb.tn(mk("matmul", bank[:, 0:NT],
                                 pb(woff + k * 128, 128),
                                 pb(XN2 + k * NT, NT),
                                 start=(k == 0), stop=(k == DCH - 1)),
                              wait=w, inc=(s_mm if k == DCH - 1 else None))
                    dst = h_xi if e < 16 else h_z
                    slot = EV + (e % 4) * NT
                    kb.v(mk("tensor_copy", out=pb(slot, NT),
                            in_=bank[:, 0:NT]),
                         wait=[(s_mm, s_mm.n), (s_st, s_st.n)], inc=s_free)
                    kb.dma("sync", dst.ap()[e % 16, :, t0:t0 + NT],
                           pb(slot, NT), s_st, wait=[(s_free, s_free.n)])
            XS, BB, CBF, OW, XPJ, XPB, OH = 0, 4096, 8192, 12288, 16384, 17152, 17408
            kb.dma("sync",
                   pb(XPJ, DICH * 48).rearrange("p (t j) -> p t j", j=48),
                   bass.AP(h_xprojT, 0, [[48, 128], [128 * 48, DICH],
                                         [1, 48]]),
                   s_ld, wait=kb.fence_waits())
            kb.dma("sync", POOL[0:48, OH:OH + 33 * 128], onehot[:, :], s_ld)
            wxp = s_ld.n
            for b in range(B):
                kb.v(mk("memset", state_sb[:, :], 0.0),
                     wait=[(s_free, s_free.n), (s_mm, s_mm.n)], inc=s_v)
                for ch in range(NTC):
                    t0 = b * S + ch * TC
                    for dd in range(DICH):
                        if ch == 0:
                            kb.v(mk("memset", ex2[:, 0:3], 0.0),
                                 wait=[(s_v, s_v.n), (s_s, s_s.n)], inc=s_v)
                            kb.dma("gpsimd", ex2[:, 3:3 + TC],
                                   h_xi.ap()[dd, :, t0:t0 + TC], s_gld,
                                   wait=[(s_v, s_v.n)])
                        else:
                            kb.dma("gpsimd", ex2[:, 0:3 + TC],
                                   h_xi.ap()[dd, :, t0 - 3:t0 + TC], s_gld,
                                   wait=[(s_v, s_v.n), (s_s, s_s.n)])
                        ldxi = s_gld.n
                        kb.v(mk("tensor_scalar", out=ex0[:, 0:TC],
                                in0=ex2[:, 0:TC],
                                scalar1=cst[:, CCONV_W + dd:CCONV_W + dd + 1],
                                scalar2=None, op0=OP.mult),
                             wait=[(s_gld, ldxi)], inc=s_v)
                        for k in range(1, 4):
                            kb.v(mk("scalar_tensor_tensor", out=ex0[:, 0:TC],
                                    in0=ex2[:, k:k + TC],
                                    scalar=cst[:, CCONV_W + k * 16 + dd:
                                               CCONV_W + k * 16 + dd + 1],
                                    in1=ex0[:, 0:TC],
                                    op0=OP.mult, op1=OP.add), inc=s_v)
                        kb.sc(mk("activation", out=pb(XS + dd * TC, TC),
                                 in_=ex0[:, 0:TC], func=AF.Silu,
                                 bias=cst[:, CCONV_B + dd:CCONV_B + dd + 1]),
                              wait=[(s_v, s_v.n)], inc=s_s)
                    xsv = s_s.n
                    for k in range(DICH):
                        w = [(s_s, xsv), (s_ld, wxp), (s_free, s_free.n),
                             (s_v, s_v.n)] if k == 0 else []
                        kb.tn(mk("matmul", PS_C[0:48, 0:TC],
                                 pb(XPJ + k * 48, 48),
                                 pb(XS + k * TC, TC),
                                 start=(k == 0), stop=(k == DICH - 1)),
                              wait=w, inc=(s_mm if k == DICH - 1 else None))
                    kb.v(mk("tensor_copy", out=POOL[0:48, XPB:XPB + TC],
                            in_=PS_C[0:48, 0:TC]),
                         wait=[(s_mm, s_mm.n)], inc=s_v)
                    xpv = s_v.n
                    for n in range(DS):
                        for row, off in ((1 + n, BB), (17 + n, CBF)):
                            kb.tn(mk("matmul", PS_B[:, 0:TC],
                                     POOL[0:48, OH + row * 128:
                                          OH + (row + 1) * 128],
                                     POOL[0:48, XPB:XPB + TC],
                                     start=True, stop=True),
                                  wait=[(s_v, xpv), (s_s, s_s.n)], inc=s_mm)
                            kb.sc(mk("activation",
                                     out=pb(off + n * TC, TC),
                                     in_=PS_B[:, 0:TC], func=AF.Copy),
                                  wait=[(s_mm, s_mm.n)], inc=s_s)
                    bcv = s_s.n
                    kb.tn(mk("matmul", PS_B[:, 0:TC],
                             POOL[0:48, OH:OH + 128],
                             POOL[0:48, XPB:XPB + TC], start=True, stop=True),
                          wait=[(s_s, bcv)], inc=s_mm)
                    xp0mm = s_mm.n
                    for dd in range(DICH):
                        kb.sc(mk("activation", out=ex0[:, 0:TC],
                                 in_=PS_B[:, 0:TC], func=AF.Exp,
                                 scale=cst[:, CDT_W + dd:CDT_W + dd + 1],
                                 bias=cst[:, CDT_B + dd:CDT_B + dd + 1]),
                              wait=[(s_mm, xp0mm), (s_v, s_v.n)], inc=s_s)
                        kb.sc(mk("activation", out=ex0[:, 0:TC],
                                 in_=ex0[:, 0:TC], func=AF.Ln,
                                 bias=cst[:, C_ONE:C_ONE + 1]), inc=s_s)
                        dv = s_s.n
                        kb.v(mk("tensor_mul", out=ex1[:, 0:TC],
                                in0=ex0[:, 0:TC], in1=pb(XS + dd * TC, TC)),
                             wait=[(s_s, dv)], inc=s_v)
                        for n in range(DS):
                            kb.sc(mk("activation", out=sq0[:, 0:TC],
                                     in_=ex0[:, 0:TC], func=AF.Exp,
                                     scale=cst[:, CA + n:CA + n + 1]),
                                  wait=[(s_v, s_v.n)], inc=s_s)
                            kb.v(mk("tensor_mul", out=sq1[:, 0:TC],
                                    in0=ex1[:, 0:TC],
                                    in1=pb(BB + n * TC, TC)),
                                 wait=[(s_s, s_s.n)], inc=s_v)
                            sidx = dd * DS + n
                            kb.v(mk("tensor_tensor_scan", out=sq1[:, 0:TC],
                                    data0=sq0[:, 0:TC], data1=sq1[:, 0:TC],
                                    initial=state_sb[:, sidx:sidx + 1],
                                    op0=OP.mult, op1=OP.add), inc=s_v)
                            kb.v(mk("tensor_copy",
                                    out=state_sb[:, sidx:sidx + 1],
                                    in_=sq1[:, TC - 1:TC]), inc=s_v)
                            kb.v(mk("tensor_mul", out=sq1[:, 0:TC],
                                    in0=sq1[:, 0:TC],
                                    in1=pb(CBF + n * TC, TC)), inc=s_v)
                            if n == 0:
                                kb.v(mk("tensor_copy", out=ex2[:, 0:TC],
                                        in_=sq1[:, 0:TC]), inc=s_v)
                            else:
                                kb.v(mk("tensor_add", out=ex2[:, 0:TC],
                                        in0=ex2[:, 0:TC], in1=sq1[:, 0:TC]),
                                     inc=s_v)
                        kb.dma("gpsimd", sq0[:, 0:TC],
                               h_z.ap()[dd, :, t0:t0 + TC], s_gld,
                               wait=[(s_v, s_v.n)])
                        kb.sc(mk("activation", out=sq0[:, 0:TC],
                                 in_=sq0[:, 0:TC], func=AF.Silu),
                              wait=[(s_gld, s_gld.n)], inc=s_s)
                        kb.v(mk("scalar_tensor_tensor", out=ex2[:, 0:TC],
                                in0=pb(XS + dd * TC, TC),
                                scalar=cst[:, CD_D + dd:CD_D + dd + 1],
                                in1=ex2[:, 0:TC], op0=OP.mult, op1=OP.add),
                             wait=[(s_s, s_s.n)], inc=s_v)
                        kb.v(mk("tensor_mul", out=pb(XS + dd * TC, TC),
                                in0=ex2[:, 0:TC], in1=sq0[:, 0:TC]), inc=s_v)
                    y2v = s_v.n
                    sd = emit_stats(t0, TC)
                    emit_mu_r(TC, sd)
                    kb.v(mk("tensor_scalar", out=ex1[:, 0:TC],
                            in0=r_b[:, 0:TC], scalar1=1.0, scalar2=None,
                            op0=OP.add), inc=s_v)
                    kb.v(mk("tensor_scalar", out=mr_b[:, 0:TC],
                            in0=mr_b[:, 0:TC], scalar1=-1.0, scalar2=None,
                            op0=OP.mult), inc=s_v)
                    resv = s_v.n
                    for dmo in range(DCH):
                        kb.dma("sync",
                               pb3(OW + (dmo % 2) * 2048, 16),
                               thbm(h_outwT, dmo * DICH, DICH),
                               s_ld, wait=[(s_mm, s_mm.n)])
                        ldo = s_ld.n
                        bank = PS_C if dmo % 2 == 0 else PS_B
                        for k in range(DICH):
                            w = []
                            if k == 0:
                                w = [(s_ld, ldo), (s_v, resv),
                                     (s_free, s_free.n)]
                            kb.tn(mk("matmul", bank[:, 0:TC],
                                     pb(OW + (dmo % 2) * 2048 + k * 128,
                                        128),
                                     pb(XS + k * TC, TC),
                                     start=(k == 0), stop=(k == DICH - 1)),
                                  wait=w,
                                  inc=(s_mm if k == DICH - 1 else None))
                        kb.v(mk("tensor_mul", out=ex0[:, 0:TC],
                                in0=xres[:, dmo * T + t0:dmo * T + t0 + TC],
                                in1=ex1[:, 0:TC]),
                             wait=[(s_mm, s_mm.n)], inc=s_v)
                        kb.v(mk("tensor_add", out=ex0[:, 0:TC],
                                in0=ex0[:, 0:TC], in1=mr_b[:, 0:TC]),
                             inc=s_v)
                        kb.v(mk("tensor_add",
                                out=xres[:, dmo * T + t0:dmo * T + t0 + TC],
                                in0=ex0[:, 0:TC], in1=bank[:, 0:TC]),
                             inc=s_free)

        def emit_attn():
            QY, KV = 0, 16384
            VP = KV + 4096
            KX = KV + 8192
            VX = KV + 8320
            WXT = KV + 8448
            XN_A = 8192
            fw = kb.fence_waits()
            kb.sy(mk("nop"), wait=fw)
            for b in range(B):
                bt0 = b * S
                for k in range(DCH):
                    kb.dma("sync", pb(WXT + k * 128, 128),
                           h_kvd.ap()[k * 128:(k + 1) * 128, :], s_ld,
                           wait=[(s_mm, s_mm.n), (s_v, s_v.n)]
                           if k == 0 else [])
                wkv = s_ld.n
                for tt4 in range(4):
                    t0 = bt0 + tt4 * NT
                    sd = emit_stats(t0, NT)
                    emit_mu_r(NT, sd)
                    lnv = emit_xn_bf16(t0, NT, XN_A, NT)
                    for k in range(DCH):
                        w = [(s_v, lnv), (s_ld, wkv), (s_free, s_free.n)] \
                            if k == 0 else []
                        kb.tn(mk("matmul", PS_C[:, 0:NT],
                                 pb(WXT + k * 128, 128),
                                 pb(XN_A + k * NT, NT),
                                 start=(k == 0), stop=(k == DCH - 1)),
                              wait=w, inc=(s_mm if k == DCH - 1 else None))
                    kb.v(mk("tensor_copy", out=pb(KV + tt4 * NT, NT),
                            in_=PS_C[:, 0:NT]),
                         wait=[(s_mm, s_mm.n)], inc=s_free)
                latv = s_free.n
                if ATTN_PH < 2:
                    continue
                for half in range(2):
                    ht0 = bt0 + half * 1024
                    for tt2 in range(2):
                        t0 = ht0 + tt2 * NT
                        sd = emit_stats(t0, NT)
                        emit_mu_r(NT, sd)
                        lnv = emit_xn_bf16(t0, NT, XN_A, NT)
                        for hc in range(DCH):
                            for k in range(DCH):
                                kb.dma("sync", pb(VP + k * 128, 128),
                                       h_wq.ap()[k * 128:(k + 1) * 128,
                                                 hc * 128:(hc + 1) * 128],
                                       s_ld,
                                       wait=[(s_mm, s_mm.n)] if k == 0
                                       else [])
                            ldq = s_ld.n
                            for k in range(DCH):
                                w = [(s_ld, ldq), (s_v, lnv),
                                     (s_free, s_free.n)] if k == 0 else []
                                kb.tn(mk("matmul", PS_B[:, 0:NT],
                                         pb(VP + k * 128, 128),
                                         pb(XN_A + k * NT, NT),
                                         start=(k == 0),
                                         stop=(k == DCH - 1)),
                                      wait=w,
                                      inc=(s_mm if k == DCH - 1 else None))
                            kb.v(mk("tensor_copy",
                                    out=pb(QY + hc * 1024 + tt2 * NT, NT),
                                    in_=PS_B[:, 0:NT]),
                                 wait=[(s_mm, s_mm.n)], inc=s_free)
                    qv = s_free.n
                    if ATTN_PH < 3:
                        continue
                    for hp in range(8):
                        kb.dma("sync", pb(KX, 128),
                               h_kvu.ap()[:, hp * 128:(hp + 1) * 128], s_ld,
                               wait=[(s_mm, s_mm.n)])
                        kb.dma("sync", pb(VX, 128),
                               h_kvu.ap()[:, 1024 + hp * 128:
                                          1024 + (hp + 1) * 128], s_ld)
                        ldkvu = s_ld.n
                        if ATTN_SUB < 1:
                            continue
                        for tt4 in range(4):
                            w = [(s_free, max(qv, latv)), (s_v, s_v.n),
                                 (s_ld, ldkvu)] \
                                if tt4 == 0 else [(s_v, s_v.n)]
                            kb.tn(mk("matmul", PS_B[:, 0:NT],
                                     pb(KX, 128),
                                     pb(KV + tt4 * NT, NT),
                                     start=True, stop=True),
                                  wait=w, inc=s_mm)
                            if ATTN_SUB >= 2:
                                kb.v(mk("tensor_copy",
                                        out=pb(KV + 2048 + tt4 * NT, NT),
                                        in_=PS_B[:, 0:NT]),
                                     wait=[(s_mm, s_mm.n)], inc=s_v)
                        if ATTN_SUB < 3:
                            continue
                        for tt4 in range(4):
                            kb.tn(mk("matmul", PS_B[:, 0:NT],
                                     pb(VX, 128),
                                     pb(KV + tt4 * NT, NT),
                                     start=True, stop=True),
                                  wait=[(s_v, s_v.n)], inc=s_mm)
                            kb.v(mk("tensor_copy",
                                    out=pb(KV + 4096 + tt4 * NT, NT),
                                    in_=PS_B[:, 0:NT]),
                                 wait=[(s_mm, s_mm.n)], inc=s_v)
                        if ATTN_SUB < 3:
                            continue
                        for w0 in (0, 8):
                            for j in range(8):
                                kc = w0 + j
                                w = [(s_v, s_v.n)] if j == 0 else []
                                kb.tn(mk("transpose",
                                         PS_T[:, j * 128:(j + 1) * 128],
                                         pb(KV + 4096 + kc * 128, 128),
                                         ident_bf_sb[:, :]), wait=w,
                                      inc=s_mm)
                            kb.v(mk("tensor_copy",
                                    out=pb(KV + 6144 + w0 * 128, 1024),
                                    in_=PS_T[:, 0:1024]),
                                 wait=[(s_mm, s_mm.n)], inc=s_v)
                        kvv = s_v.n
                        if ATTN_PH < 4:
                            continue
                        for h01 in range(2):
                            o = 64 * h01
                            for kc in range(16):
                                kb.sc(mk("activation",
                                         out=sq0[0:64, 0:128],
                                         in_=POOL[o:o + 64,
                                                  KV + 2048 + kc * 128:
                                                  KV + 2048 + kc * 128
                                                  + 128],
                                         func=AF.Square),
                                      wait=[(s_v, kvv), (s_mm, s_mm.n)]
                                      if kc == 0 else [(s_mm, s_mm.n)],
                                      inc=s_s)
                                kb.tn(mk("matmul", PS_C[:, kc:kc + 1],
                                         sq0[0:64, 0:128],
                                         ones_f_sb[0:64, 0:1],
                                         start=True, stop=True),
                                      wait=[(s_s, s_s.n)] +
                                      ([(s_free, s_free.n)] if kc == 0
                                       else []), inc=s_mm)
                            kb.v(mk("tensor_copy", out=ex0[:, 0:16],
                                    in_=PS_C[:, 0:16]),
                                 wait=[(s_mm, s_mm.n)], inc=s_v)
                            kb.sc(mk("activation", out=ex0[:, 0:16],
                                     in_=ex0[:, 0:16], func=AF.Sqrt),
                                  wait=[(s_v, s_v.n)], inc=s_s)
                            kb.v(mk("reduce_max", out=col["kmax"][:, 0:1],
                                    in_=ex0[:, 0:16], axis=AX.X),
                                 wait=[(s_s, s_s.n)], inc=s_v)
                            crosspart_max(col["kmax"])
                            bcast11(col["kmax"])
                            if ATTN_PH < 5:
                                continue
                            for qtl in range(8):
                                qg = half * 8 + qtl
                                kl = (qg + 1) * 128
                                qoff = QY + hp * 1024 + qtl * 128
                                nks = (kl + 511) // 512
                                for ks in range(nks):
                                    n_ = min(512, kl - ks * 512)
                                    w = [(s_s, s_s.n), (s_v, s_v.n),
                                         (s_free, s_free.n)] \
                                        if ks == 0 else []
                                    kb.tn(mk("matmul",
                                             PS_A[:, ks * 512:
                                                  ks * 512 + n_],
                                             POOL[o:o + 64,
                                                  qoff:qoff + 128],
                                             POOL[o:o + 64,
                                                  KV + 2048 + ks * 512:
                                                  KV + 2048 + ks * 512
                                                  + n_],
                                             start=True, stop=True),
                                          wait=w, inc=s_mm)
                                mmqk = s_mm.n
                                kb.v(mk("tensor_add",
                                        out=PS_A[:, qg * 128:
                                                 qg * 128 + 128],
                                        in0=PS_A[:, qg * 128:
                                                 qg * 128 + 128],
                                        in1=ltneg_sb[:, :]),
                                     wait=[(s_mm, mmqk)], inc=s_v)
                                kb.sc(mk("activation",
                                         out=sq1[0:64, 0:128],
                                         in_=POOL[o:o + 64,
                                                  qoff:qoff + 128],
                                         func=AF.Square),
                                      wait=[(s_v, s_v.n)], inc=s_s)
                                kb.tn(mk("matmul", PS_C[:, 20:21],
                                         sq1[0:64, 0:128],
                                         ones_f_sb[0:64, 0:1],
                                         start=True, stop=True),
                                      wait=[(s_s, s_s.n)], inc=s_mm)
                                kb.v(mk("tensor_copy",
                                        out=col["qn"][:, 0:1],
                                        in_=PS_C[:, 20:21]),
                                     wait=[(s_mm, s_mm.n)], inc=s_v)
                                kb.sc(mk("activation",
                                         out=col["qn"][:, 0:1],
                                         in_=col["qn"][:, 0:1],
                                         func=AF.Sqrt),
                                      wait=[(s_v, s_v.n)], inc=s_s)
                                kb.v(mk("tensor_mul",
                                        out=col["negM"][:, 0:1],
                                        in0=col["qn"][:, 0:1],
                                        in1=col["kmax"][:, 0:1]),
                                     wait=[(s_s, s_s.n)], inc=s_v)
                                kb.v(mk("tensor_scalar",
                                        out=col["negM"][:, 0:1],
                                        in0=col["negM"][:, 0:1],
                                        scalar1=-0.125, scalar2=None,
                                        op0=OP.mult), inc=s_v)
                                kb.sc(mk("activation", out=pb(VP, kl),
                                         in_=PS_A[:, 0:kl], func=AF.Exp,
                                         scale=cst[:, C_I8:C_I8 + 1],
                                         bias=col["negM"][:, 0:1],
                                         accum_out=col["lcol"][:, 0:1]),
                                      wait=[(s_v, s_v.n)], inc=s_s)
                                kb.sc(mk("activation",
                                         out=col["linv"][:, 0:1],
                                         in_=col["lcol"][:, 0:1],
                                         func=AF.Ln), inc=s_s)
                                kb.sc(mk("activation",
                                         out=col["linv"][:, 0:1],
                                         in_=col["linv"][:, 0:1],
                                         func=AF.Exp,
                                         scale=cst[:, C_NEG1:C_NEG1 + 1]),
                                      inc=s_s)
                                kb.v(mk("engine_nop"), wait=[(s_s, s_s.n)],
                                     inc=s_v)
                                kb.v(mk("tensor_scalar", out=pb(VP, kl),
                                        in0=pb(VP, kl),
                                        scalar1=col["linv"][:, 0:1],
                                        scalar2=None, op0=OP.mult),
                                     inc=s_v)
                                nkc = kl // 128
                                if ATTN_PH < 6:
                                    continue
                                for w0 in range(0, nkc, 8):
                                    nw = min(8, nkc - w0)
                                    for j in range(nw):
                                        kc = w0 + j
                                        w = [(s_v, s_v.n)] if j == 0 else []
                                        kb.tn(mk("transpose",
                                                 PS_T[:, j * 128:
                                                      (j + 1) * 128],
                                                 pb(VP + kc * 128, 128),
                                                 ident_bf_sb[:, :]),
                                              wait=w, inc=s_mm)
                                    kb.v(mk("tensor_copy",
                                            out=pb(WXT, nw * 128),
                                            in_=PS_T[:, 0:nw * 128]),
                                         wait=[(s_mm, s_mm.n)], inc=s_v)
                                    ptv = s_v.n
                                    for j in range(nw):
                                        kc = w0 + j
                                        kb.tn(mk("matmul",
                                                 PS_C[0:64, 0:128],
                                                 POOL[:, KV + 6144 +
                                                      kc * 128 + o:
                                                      KV + 6144 +
                                                      kc * 128 + o + 64],
                                                 pb(WXT + j * 128, 128),
                                                 start=(kc == 0),
                                                 stop=(kc == nkc - 1)),
                                              wait=[(s_v, ptv)] if j == 0
                                              else [],
                                              inc=(s_mm if kc == nkc - 1
                                                   else None))
                                kb.sc(mk("activation",
                                         out=POOL[o:o + 64,
                                                  QY + 8192 + hp * 1024 +
                                                  qtl * 128:
                                                  QY + 8192 + hp * 1024 +
                                                  qtl * 128 + 128],
                                         in_=PS_C[0:64, 0:128],
                                         func=AF.Copy),
                                      wait=[(s_mm, s_mm.n)], inc=s_s)
                    yv = s_s.n
                    if ATTN_PH < 7:
                        continue
                    for tt2 in range(2):
                        t0 = ht0 + tt2 * NT
                        for d in range(DCH):
                            for k in range(DCH):
                                kb.dma("sync",
                                       pb(WXT + k * 128, 128),
                                       h_wout.ap()[k * 128:(k + 1) * 128,
                                                   d * 128:(d + 1) * 128],
                                       s_ld,
                                       wait=[(s_mm, s_mm.n), (s_v, s_v.n)]
                                       if k == 0 else [])
                            ldo = s_ld.n
                            bank = PS_C if d % 2 == 0 else PS_B
                            for k in range(DCH):
                                w = []
                                if k == 0:
                                    w = [(s_ld, ldo), (s_s, yv),
                                         (s_v, s_v.n)]
                                kb.tn(mk("matmul", bank[:, 0:NT],
                                         pb(WXT + k * 128, 128),
                                         pb(QY + 8192 + k * 1024 +
                                            tt2 * NT, NT),
                                         start=(k == 0),
                                         stop=(k == DCH - 1)),
                                      wait=w,
                                      inc=(s_mm if k == DCH - 1 else None))
                            kb.v(mk("tensor_add",
                                    out=xres[:, xsl(d, t0, NT)],
                                    in0=xres[:, xsl(d, t0, NT)],
                                    in1=bank[:, 0:NT]),
                                 wait=[(s_mm, s_mm.n)], inc=s_free)

        for st in stages:
            if st in ("s1", "s3"):
                emit_ffn0(st)
            elif st == "s2":
                emit_mamba()
            elif st in ("s4", "s6"):
                emit_ffn1(st)
            elif st == "s5":
                emit_attn()

        fw = kb.fence_waits()
        for b in range(B):
            for c in range(DCH):
                kb.dma("sync", out_d[b, c * 128:(c + 1) * 128, :],
                       xres[:, xsl(c, b * S, S)], s_st,
                       wait=fw if (b == 0 and c == 0) else [])
        kb.sy(mk("nop"), wait=[(s_st, s_st.n), (s_ld, s_ld.n),
                               (s_gld, s_gld.n)])

        @block.sync
        def _(sync):
            kb.replay("sync", sync)

        @block.tensor
        def _(tensor):
            kb.replay("tensor", tensor)

        @block.vector
        def _(vector):
            kb.replay("vector", vector)

        @block.scalar
        def _(scalar):
            kb.replay("scalar", scalar)

        @block.gpsimd
        def _(g):
            kb.replay("gpsimd", g)

    return nc


def _consts():
    lt = np.zeros((128, 128), np.float32)
    for i in range(128):
        lt[i, i + 1:] = -1e5
    oh = np.zeros((48, 33 * 128), ml_dtypes.bfloat16)
    for r in range(33):
        oh[r, r * 128:(r + 1) * 128] = 1
    return {
        "c_ident_bf": np.eye(128, dtype=ml_dtypes.bfloat16),
        "c_ones_f": np.ones((128, 128), np.float32),
        "c_ident_f": np.eye(128, dtype=np.float32),
        "c_ltneg": lt,
        "c_onehot": oh,
    }


_LAST = None
_WNAMES = ["L0_ffn_w_gate", "L0_ffn_w_up", "L0_ffn_w_down", "L0_ffn_clip_x",
           "L0_ffn_clip_down", "L0_attn_ln_g", "L0_attn_ln_b",
           "L0_attn_in_w", "L0_attn_conv_w", "L0_attn_conv_b",
           "L0_attn_xproj_w", "L0_attn_dt_w", "L0_attn_dt_b",
           "L0_attn_A_log", "L0_attn_D", "L0_attn_out_w",
           "L1_attn_w_kv_down", "L1_attn_w_kv_up", "L1_attn_w_q",
           "L1_attn_w_out", "L1_ffn_weight"]


def kernel(**inputs):
    global _LAST
    x = np.asarray(inputs["L0_attn_x"])
    nc = build_program(STAGES, DEBUG)
    consts = _consts()
    in_maps = []
    for core in range(NCORES):
        xb = x[core * B:(core + 1) * B]
        m = {"x": np.ascontiguousarray(xb.transpose(0, 2, 1))}
        for w in _WNAMES:
            m[w] = np.asarray(inputs[w])
        m.update(consts)
        in_maps.append(m)
    res = run_bass_kernel_spmd(nc, in_maps, core_ids=list(range(NCORES)))
    _LAST = res.results
    out = np.empty((BATCH, S, D), np.float32)
    for core in range(NCORES):
        o = np.asarray(res.results[core]["out"])
        out[core * B:(core + 1) * B] = o.transpose(0, 2, 1)
    return out
